# version 02
# baseline (speedup 1.0000x reference)
import sys
sys.path.insert(0, "/opt/trn_rl_repo")
import numpy as np
import ml_dtypes
import concourse.bass as bass
import concourse.mybir as mybir
from concourse.bass_utils import run_bass_kernel_spmd

F32 = mybir.dt.float32
BF16 = mybir.dt.bfloat16
AF = mybir.ActivationFunctionType
OP = mybir.AluOpType
AX = mybir.AxisListType

T, D, HEADS, L, DH = 2048, 1024, 16, 64, 4096
NT, NF, NH = 16, 8, 32
CH, NCH, NST = 512, 4, 4
SCALE = 0.125

ENGINES = ("sync", "gpsimd", "tensor", "vector", "scalar")


class Prog:
    def __init__(self):
        self.steps = {e: [] for e in ENGINES}
        self.cnt = {e: 0 for e in ENGINES}

    def on(self, eng, fn, deps=(), inc=1):
        w = {}
        for d in deps:
            if d is None:
                continue
            de, dv = d
            if de == eng:
                continue
            w[de] = max(w.get(de, 0), dv)
        self.cnt[eng] += inc
        self.steps[eng].append((w, fn, inc))
        return (eng, self.cnt[eng])

    def fence(self, eng):
        return (eng, self.cnt[eng])

    def barrier(self):
        return [self.fence(e) for e in ENGINES]


class Bank:
    def __init__(self, t):
        self.t = t
        self.w = None
        self.r = None


def build(phases="FAF", dump=None):
    nc = bass.Bass("TRN2", target_bir_lowering=False)

    def din(name, shape, dtype=F32):
        return nc.dram_tensor(name, shape, dtype, kind="ExternalInput").ap()

    x_h = din("x", [T, D])
    wqk_h = din("wqk", [16, 128, 1024])
    wvm_h = din("wvm", [8, 128, 1024])
    wo_h = din("wo", [2, 8, 128, 512])
    w1_h = din("w1", [32, 128, 1024])
    w2_h = din("w2", [2, 32, 128, 512])
    b1c_h = din("b1c", [128, 32])
    b2r_h = din("b2r", [1, 1024])
    wrb_h = din("wrb", [128, 1024])
    brb_h = din("brb", [128, 1])
    identb_h = din("identb", [128, 128], BF16)
    trimask_h = din("trimask", [128, 512], BF16)
    ones1f_h = din("ones1f", [1, 128])
    ones2_h = din("ones2", [2, 128], BF16)
    ind2T_h = din("ind2T", [2, 128], BF16)
    ind2_h = din("ind2", [128, 2], BF16)
    onesK1_h = din("onesK1", [1, 128], BF16)

    out_h = nc.dram_tensor("out", [T, D], F32, kind="ExternalOutput").ap()
    dbg_h = None
    if dump is not None:
        dbg_h = nc.dram_tensor("dbg", [128, 16384], F32, kind="ExternalOutput").ap()

    from contextlib import ExitStack
    with ExitStack() as ctx:
        E = ctx.enter_context
        xres = E(nc.sbuf_tensor("xres", [128, NT * 1024], F32))
        xlnT = E(nc.sbuf_tensor("xlnT", [128, NF * 2048], BF16))
        xlnb = E(nc.sbuf_tensor("xlnb", [128, 16384], BF16))
        hT = E(nc.sbuf_tensor("hT", [128, 16384], BF16))
        wb1 = E(nc.sbuf_tensor("wb1", [128, 2048], BF16))
        wb2 = E(nc.sbuf_tensor("wb2", [128, 1024], BF16))
        wvo = E(nc.sbuf_tensor("wvo", [128, 1024], BF16))
        scr = E(nc.sbuf_tensor("scr", [128, 2048], F32))
        sden = E(nc.sbuf_tensor("sden", [2, 512], BF16))
        ssb = E(nc.sbuf_tensor("ssb", [128, 128], BF16))
        amask = E(nc.sbuf_tensor("amask", [128, 1024], BF16))
        srow = E(nc.sbuf_tensor("srow", [1, 2048], F32))
        stats = E(nc.sbuf_tensor("stats", [128, 12], F32))
        smv = E(nc.sbuf_tensor("smv", [128, 2], F32))
        sstd = E(nc.sbuf_tensor("sstd", [128, 1], F32))
        seps = E(nc.sbuf_tensor("seps", [128, 1], F32))
        mus = E(nc.sbuf_tensor("mus", [128, 16], F32))
        rsd = E(nc.sbuf_tensor("rsd", [128, 16], F32))
        scor = E(nc.sbuf_tensor("scor", [128, 16], F32))
        nsc = E(nc.sbuf_tensor("nsc", [128, 16], F32))
        csg = E(nc.sbuf_tensor("csg", [128, 16], F32))
        hfm = E(nc.sbuf_tensor("hfm", [128, 16], F32))
        wbar = E(nc.sbuf_tensor("wbar", [128, 1], F32))
        stmp = E(nc.sbuf_tensor("stmp", [128, 1], F32))
        stmp6 = E(nc.sbuf_tensor("stmp6", [128, 1], F32))
        sdot = E(nc.sbuf_tensor("sdot", [128, 1], F32))
        sacc = E(nc.sbuf_tensor("sacc", [128, 1], F32))
        carry = E(nc.sbuf_tensor("carry", [128, 8], F32))
        identb = E(nc.sbuf_tensor("identb_s", [128, 128], BF16))
        trimask = E(nc.sbuf_tensor("trimask_s", [128, 512], BF16))
        ones1f = E(nc.sbuf_tensor("ones1f_s", [1, 128], F32))
        ones2 = E(nc.sbuf_tensor("ones2_s", [2, 128], BF16))
        ind2T = E(nc.sbuf_tensor("ind2T_s", [2, 128], BF16))
        ind2 = E(nc.sbuf_tensor("ind2_s", [128, 2], BF16))
        onesK1 = E(nc.sbuf_tensor("onesK1_s", [1, 128], BF16))
        b1c = E(nc.sbuf_tensor("b1c_s", [128, 32], F32))
        b2rb = E(nc.sbuf_tensor("b2rb_s", [1, 1024], BF16))
        wrb = E(nc.sbuf_tensor("wrb_s", [128, 1024], F32))
        brb = E(nc.sbuf_tensor("brb_s", [128, 1], F32))

        pbT_t = E(nc.psum_tensor("pbT", [128, 1024], BF16))
        pb = [E(nc.psum_tensor(f"pb{i}", [128, 512], F32)) for i in range(7)]
        sems = {e: E(nc.semaphore(f"sem_{e}")) for e in ENGINES}
        block = E(nc.Block())

        p = Prog()
        bT = Bank(pbT_t)
        b = [Bank(t) for t in pb]
        bEv = [b[0], b[1]]
        bA = b[2]
        bY = [b[3], b[4]]
        bS = [b[5], b[6]]
        bF2 = [b[2], b[3], b[4], b[5]]
        bDel = [b[0], b[1], b[2], b[3]]

        av = hT[:, 0:4096]
        akt = hT[:, 4096:8192]
        akT = hT[:, 8192:12288]
        aqs = hT[:, 12288:16384]
        ayT = xlnb[:, 0:4096]
        vT = xlnb[:, 4096:8192]
        alns = xlnb[:, 8192:10240]

        tok_x = [None] * NT

        def DMA(out, in_, deps=(), q="sync"):
            return p.on(q, lambda e, o=out, i=in_: e.dma_start(out=o, in_=i), deps, inc=16)

        def TOK(eng, fn, deps=()):
            return p.on(eng, fn, deps)

        for i in range(4):
            src = x_h.rearrange("(t p) f -> p t f", p=128)[:, 4 * i:4 * i + 4, :]
            dst = xres[:, 4 * i * 1024:(4 * i + 4) * 1024].rearrange("p (t f) -> p t f", f=1024)
            tk = DMA(dst, src)
            for j in range(4):
                tok_x[4 * i + j] = tk
        for dst, srch in ((identb, identb_h), (trimask, trimask_h), (ones1f, ones1f_h),
                          (ones2, ones2_h), (ind2, ind2_h), (onesK1, onesK1_h), (ind2T, ind2T_h),
                          (b1c, b1c_h), (b2rb, b2r_h), (wrb, wrb_h), (brb, brb_h)):
            DMA(dst[:], srch, q="gpsimd")
        tok_consts = p.fence("gpsimd")
        t_eps = TOK("vector", lambda e: e.memset(seps[:], 1e-5))
        t_eps6 = TOK("vector", lambda e: e.memset(stmp6[:], 1e-6))
        def _wbar(e):
            e.tensor_reduce(wbar[:], wrb[:], AX.X, OP.add)
            return e.drain()
        t_wbar = TOK("vector", _wbar, [tok_consts])

        xlnT_f = xlnT[:].rearrange("p (f t) -> p f t", f=NF)
        pbT_f = pbT_t[:].rearrange("p (f q) -> p f q", f=8)
        pbT_f3 = lambda: pbT_t[:].rearrange("p (f q) -> p f q", f=8)

        def ffn_sublayer(upto="all", no_tr=False):
            bar = p.barrier()
            t_norm = [None] * NT
            t_tr = [None] * NT
            prev_s4 = None
            for tt in range(NT):
                xt = xres[:, tt * 1024:(tt + 1) * 1024]
                dst = xlnb[:, tt * 1024:(tt + 1) * 1024]
                d = [tok_x[tt]] + (bar if tt == 0 else [])
                s1 = TOK("vector", lambda e, xt=xt: e.bn_stats(stats[:, 0:6], xt[:, 0:512]), d)
                def _bns2(e, xt=xt):
                    e.bn_stats(stats[:, 6:12], xt[:, 512:1024])
                    return e.drain()
                TOK("vector", _bns2)
                def _aggr(e):
                    e.bn_aggr(smv[:], stats[:])
                    return e.drain()
                s3 = TOK("vector", _aggr, [prev_s4])
                s4 = TOK("scalar", lambda e: e.activation(sstd[:], smv[:, 1:2], AF.Sqrt,
                                                          bias=seps[:], scale=1.0),
                         [s3, t_eps] + (bar if tt == 0 else []))
                prev_s4 = s4
                def _recip(e, tt=tt):
                    e.reciprocal(sdot[:], sstd[:])
                    e.drain()
                    return e.tensor_copy(rsd[:, tt:tt + 1], sdot[:])
                s5 = TOK("vector", _recip, [s4])
                s6 = TOK("vector", lambda e, tt=tt: e.tensor_copy(mus[:, tt:tt + 1], smv[:, 0:1]))
                t_norm[tt] = TOK("vector", lambda e, xt=xt, dst=dst, tt=tt: e.tensor_scalar(
                    dst, xt, mus[:, tt:tt + 1], rsd[:, tt:tt + 1], OP.subtract, OP.mult))
                def _ttr(e, xt=xt, tt=tt):
                    e.tensor_tensor_reduce(
                        out=scr[:, 0:1024], in0=xt, in1=wrb[:], scale=1.0, scalar=0.0,
                        op0=OP.mult, op1=OP.add, accum_out=sdot[:])
                    e.drain()
                    return e.tensor_copy(scor[:, tt:tt + 1], sdot[:])
                s8 = TOK("vector", _ttr, [tok_consts])
                s9 = TOK("vector", lambda e, tt=tt: e.tensor_scalar(
                    stmp[:], mus[:, tt:tt + 1], wbar[:], None, OP.mult), [t_wbar])
                s10 = TOK("vector", lambda e, tt=tt: e.tensor_sub(
                    stmp[:], scor[:, tt:tt + 1], stmp[:]))
                s11 = TOK("vector", lambda e, tt=tt: e.tensor_scalar(
                    scor[:, tt:tt + 1], stmp[:], rsd[:, tt:tt + 1], brb[:], OP.mult, OP.add))
                TOK("vector", lambda e, tt=tt: e.tensor_scalar(
                    nsc[:, tt:tt + 1], scor[:, tt:tt + 1], -1.0, None, OP.mult))
                if no_tr:
                    t_tr[tt] = t_norm[tt]
                    continue
                mm = None
                for bi in range(NF):
                    mm = TOK("tensor", lambda e, dst=dst, bi=bi: e.transpose(
                        pbT_t[:, bi * 128:(bi + 1) * 128], dst[:, bi * 128:(bi + 1) * 128], identb[:]),
                        ([bT.r, tok_consts, t_norm[tt]] + (bar if tt == 0 else []))
                        if bi == 0 else [])
                bT.w = mm
                ev = None
                for bi in range(NF):
                    ev = TOK("scalar", lambda e, bi=bi, tt=tt: e.copy(
                        xlnT[:, bi * 2048 + tt * 128: bi * 2048 + (tt + 1) * 128],
                        pbT_t[:, bi * 128:(bi + 1) * 128]), [bT.w] if bi == 0 else [])
                bT.r = ev
                t_tr[tt] = ev

            if upto == "ln":
                return
            t_sc = p.fence("vector")
            t_row = DMA(srow[:], scor[:], [t_sc])
            t_bc = []
            for j in range(4):
                bk = bEv[j % 2]
                mmt = TOK("tensor", lambda e, j=j, bk=bk: e.matmul(
                    bk.t[:], ones1f[:], srow[:, j * 512:(j + 1) * 512], start=True, stop=True),
                    [t_row, tok_consts, bk.r])
                bk.w = mmt
                ev = TOK("scalar", lambda e, j=j, bk=bk: e.copy(
                    scr[:, j * 512:(j + 1) * 512], bk.t[:]), [bk.w, t_sc])
                bk.r = ev
                t_bc.append(ev)
            t_cnt = []
            for tt in range(NT):
                def _sgn(e, tt=tt):
                    e.activation(hT[:, 0:2048], scr[:, 0:2048], AF.Sign,
                                 bias=nsc[:, tt:tt + 1], scale=1.0, accum_out=sacc[:])
                    e.drain()
                    return e.copy(csg[:, tt:tt + 1], sacc[:])
                t_cnt.append(TOK("scalar", _sgn,
                                 t_bc + ([p.fence("vector")] if tt == 0 else [])))
            t_hfm = TOK("vector", lambda e: e.tensor_scalar(
                hfm[:], csg[:], -2.5, 0.5, OP.is_le, OP.mult), t_cnt)

            if upto == "cnt":
                return
            tok_hT_rd = [t_hfm] * NH
            t_w1_rd = [None, None]
            t_w2_rd = [None, None]
            t_epi_scr = None
            for c in range(NCH):
                t_trf = [t_tr[4 * c + j] for j in range(4)]
                tok_hT = [None] * NH
                for m in range(NH):
                    sl = m % 2
                    t_wl = DMA(wb1[:, sl * 1024:(sl + 1) * 1024], w1_h[m],
                               [t_w1_rd[sl]], q="gpsimd")
                    bk = bEv[m % 2]
                    mm = None
                    for k in range(NF):
                        mm = TOK("tensor", lambda e, k=k, sl=sl, c=c, bk=bk: e.matmul(
                            bk.t[:], wb1[:, sl * 1024 + k * 128: sl * 1024 + (k + 1) * 128],
                            xlnT[:, k * 2048 + c * 512: k * 2048 + (c + 1) * 512],
                            start=(k == 0), stop=(k == NF - 1)),
                            ([t_wl, bk.r] + t_trf) if k == 0 else [])
                    bk.w = mm
                    t_w1_rd[sl] = mm
                    ge = TOK("scalar", lambda e, m=m, bk=bk: e.activation(
                        hT[:, m * 512:(m + 1) * 512], bk.t[:], AF.Gelu,
                        bias=b1c[:, m:m + 1], scale=1.0),
                        [bk.w, tok_consts, tok_hT_rd[m]])
                    bk.r = ge
                    tok_hT[m] = ge
                tok_hT_rd = [None] * NH
                for oc in range(2):
                    last_mm = [None] * NST
                    for k2 in range(NH):
                        sl = k2 % 2
                        t_w2 = DMA(wb2[:, sl * 512:(sl + 1) * 512], w2_h[oc, k2],
                                   [t_w2_rd[sl]], q="gpsimd")
                        for m2 in range(NST):
                            bk = bF2[m2]
                            dd = []
                            if k2 == 0:
                                dd = [t_w2, bk.r, tok_hT[k2]]
                            elif m2 == 0:
                                dd = [t_w2, tok_hT[k2]]
                            last_mm[m2] = TOK("tensor", lambda e, k2=k2, m2=m2, sl=sl, bk=bk:
                                e.matmul(bk.t[:],
                                         hT[:, k2 * 512 + m2 * 128: k2 * 512 + (m2 + 1) * 128],
                                         wb2[:, sl * 512:(sl + 1) * 512],
                                         start=(k2 == 0), stop=False,
                                         skip_group_check=True), dd)
                        t_w2_rd[sl] = last_mm[NST - 1]
                        tok_hT_rd[k2] = last_mm[NST - 1]
                    for m2 in range(NST):
                        bk = bF2[m2]
                        bk.w = TOK("tensor", lambda e, oc=oc, bk=bk: e.matmul(
                            bk.t[:], onesK1[:], b2rb[:, oc * 512:(oc + 1) * 512],
                            start=False, stop=True, skip_group_check=True), [tok_consts])
                    for m2 in range(NST):
                        bk = bF2[m2]
                        tt = 4 * c + m2
                        colx = tt * 1024 + oc * 512
                        xh = xres[:, colx:colx + 512]
                        lh = xlnb[:, colx:colx + 512]
                        d1 = TOK("vector", lambda e, bk=bk, lh=lh: e.tensor_sub(
                            scr[:, 0:512], bk.t[:], lh), [bk.w, t_hfm, t_epi_scr])
                        bk.r = d1
                        TOK("vector", lambda e, xh=xh, lh=lh: e.scalar_tensor_tensor(
                            xh, lh, 0.5, xh, OP.mult, OP.add))
                        d3 = TOK("vector", lambda e, xh=xh, tt=tt: e.scalar_tensor_tensor(
                            xh, scr[:, 0:512], hfm[:, tt:tt + 1], xh, OP.mult, OP.add))
                        t_epi_scr = d3
                        if oc == 1:
                            tok_x[tt] = d3

        def attn_sublayer():
            bar = p.barrier()
            t_tr = [None] * NT
            prev_s4a = None
            for tt in range(NT):
                xt = xres[:, tt * 1024:(tt + 1) * 1024]
                sl = tt % 2
                dst = alns[:, sl * 1024:(sl + 1) * 1024]
                d = [tok_x[tt]] + (bar if tt == 0 else [])
                TOK("vector", lambda e, xt=xt: e.bn_stats(stats[:, 0:6], xt[:, 0:512]), d)
                def _bns2a(e, xt=xt):
                    e.bn_stats(stats[:, 6:12], xt[:, 512:1024])
                    return e.drain()
                TOK("vector", _bns2a)
                def _aggra(e):
                    e.bn_aggr(smv[:], stats[:])
                    return e.drain()
                s3 = TOK("vector", _aggra, [prev_s4a])
                s4 = TOK("scalar", lambda e: e.activation(sstd[:], smv[:, 1:2], AF.Sqrt,
                                                          bias=seps[:], scale=1.0),
                         [s3, t_eps] + (bar if tt == 0 else []))
                prev_s4a = s4
                def _recipa(e):
                    e.reciprocal(sstd[:], sstd[:])
                    return e.drain()
                s5 = TOK("vector", _recipa, [s4])
                wdep = [t_tr[tt - 2]] if tt >= 2 else []
                t_norm = TOK("vector", lambda e, xt=xt, dst=dst: e.tensor_scalar(
                    dst, xt, smv[:, 0:1], sstd[:], OP.subtract, OP.mult), wdep)
                mm = None
                for bi in range(NF):
                    mm = TOK("tensor", lambda e, dst=dst, bi=bi: e.transpose(
                        pbT_t[:, bi * 128:(bi + 1) * 128], dst[:, bi * 128:(bi + 1) * 128], identb[:]),
                        ([bT.r, tok_consts, t_norm] + (bar if tt == 0 else []))
                        if bi == 0 else [])
                bT.w = mm
                ev = None
                for bi in range(NF):
                    ev = TOK("scalar", lambda e, bi=bi, tt=tt: e.copy(
                        xlnT[:, bi * 2048 + tt * 128: bi * 2048 + (tt + 1) * 128],
                        pbT_t[:, bi * 128:(bi + 1) * 128]), [bT.w] if bi == 0 else [])
                bT.r = ev
                t_tr[tt] = ev

            t_zc = TOK("vector", lambda e: e.memset(carry[:], 0.0), bar)
            s_started = [False, False]
            tok_S_w = [None] * HEADS
            tok_S_r = [None] * HEADS
            t_w_rd = [None, None]
            t_w2_rd = [None, None]
            t_wo_rd = [None, None]
            t_amask_rd = [None, None]
            t_ssb_rd = [None, None]
            t_ayT_rd = None
            t_buf_rd = None
            prev_sd = None
            prev_p3 = None
            prev_qs = None

            for c in range(NCH):
                t_trf = [t_tr[4 * c + j] for j in range(4)]
                t_aqs = [None] * NF
                t_akT = [None] * NF
                for fi in range(NF):
                    mmk = mmq = None
                    for (mi, which) in ((8 + fi, "k"), (fi, "q")):
                        sl = mi % 2
                        t_wl = DMA(wb1[:, sl * 1024:(sl + 1) * 1024], wqk_h[mi],
                                   [t_w_rd[sl]], q="gpsimd")
                        bk = bEv[0] if which == "k" else bEv[1]
                        mm = None
                        for k in range(NF):
                            mm = TOK("tensor", lambda e, k=k, sl=sl, c=c, bk=bk: e.matmul(
                                bk.t[:], wb1[:, sl * 1024 + k * 128: sl * 1024 + (k + 1) * 128],
                                xlnT[:, k * 2048 + c * 512: k * 2048 + (c + 1) * 512],
                                start=(k == 0), stop=(k == NF - 1)),
                                ([t_wl, bk.r] + t_trf) if k == 0 else [])
                        bk.w = mm
                        t_w_rd[sl] = mm
                        if which == "k":
                            mmk = mm
                        else:
                            mmq = mm
                    bkk, bkq = bEv[0], bEv[1]
                    e1 = TOK("scalar", lambda e, bkk=bkk: e.activation(
                        scr[:, 0:512], bkk.t[:], AF.Exp, bias=0.0, scale=SCALE),
                        [bkk.w, prev_sd])
                    e2 = TOK("scalar", lambda e: e.activation(
                        scr[:, 0:512], scr[:, 0:512], AF.Identity,
                        bias=stmp6[:], scale=1.0), [e1, t_eps6])
                    bkk.r = e2
                    t_akT[fi] = TOK("vector", lambda e, fi=fi: e.tensor_copy(
                        akT[:, fi * 512:(fi + 1) * 512], scr[:, 0:512]),
                        [e2] + ([t_buf_rd] if fi == 0 else []))
                    TOK("vector", lambda e, fi=fi: e.tensor_tensor_scan(
                        scr[:, 512:1024], scr[:, 0:512], scr[:, 0:512], carry[:, fi:fi + 1],
                        OP.add, OP.bypass), [t_zc])
                    sd = TOK("vector", lambda e: e.drain())
                    prev_sd = sd
                    TOK("vector", lambda e, fi=fi: e.tensor_copy(
                        carry[:, fi:fi + 1], scr[:, 1023:1024]))
                    q1 = TOK("scalar", lambda e, bkq=bkq: e.copy(scr[:, 1536:2048], bkq.t[:]),
                             [bkq.w, prev_p3])
                    q2 = TOK("scalar", lambda e, fi=fi, bkq=bkq: e.activation(
                        aqs[:, fi * 512:(fi + 1) * 512], bkq.t[:], AF.Exp, bias=0.0, scale=SCALE),
                        [bkq.w] + ([t_buf_rd] if fi == 0 else []))
                    bkq.r = q2
                    mmd = TOK("tensor", lambda e, fi=fi, bkk=bkk: e.matmul(
                        bkk.t[0:2, :], ind2[:], aqs[:, fi * 512:(fi + 1) * 512],
                        start=True, stop=True, skip_group_check=True),
                        [q2, bkk.r, tok_consts])
                    bkk.w = mmd
                    de = TOK("scalar", lambda e, bkk=bkk: e.copy(sden[:], bkk.t[0:2, :]), [mmd])
                    bkk.r = de
                    mmb = TOK("tensor", lambda e, bkq=bkq: e.matmul(
                        bkq.t[:], ind2T[:], sden[:], start=True, stop=True,
                        skip_group_check=True), [de, bkq.r, tok_consts])
                    bkq.w = mmb
                    p1 = TOK("vector", lambda e, bkq=bkq: e.tensor_mul(
                        scr[:, 1024:1536], scr[:, 512:1024], bkq.t[:]),
                        [mmb, prev_qs])
                    bkq.r = p1
                    p2 = TOK("scalar", lambda e: e.activation(
                        scr[:, 1024:1536], scr[:, 1024:1536], AF.Ln, bias=0.0, scale=1.0), [p1])
                    p3 = TOK("vector", lambda e: e.scalar_tensor_tensor(
                        scr[:, 1024:1536], scr[:, 1536:2048], SCALE, scr[:, 1024:1536],
                        OP.mult, OP.subtract), [p2, q1])
                    prev_p3 = p3
                    t_aqs[fi] = TOK("scalar", lambda e, fi=fi: e.activation(
                        aqs[:, fi * 512:(fi + 1) * 512], scr[:, 1024:1536], AF.Exp,
                        bias=0.0, scale=1.0), [p3, mmd])
                    prev_qs = t_aqs[fi]
                t_vT = [None] * NF
                for fi in range(NF):
                    sl = fi % 2
                    t_wl = DMA(wb1[:, sl * 1024:(sl + 1) * 1024], wvm_h[fi],
                               [t_w_rd[sl]], q="gpsimd")
                    bk = bEv[fi % 2]
                    mm = None
                    for k in range(NF):
                        mm = TOK("tensor", lambda e, k=k, sl=sl, c=c, bk=bk: e.matmul(
                            bk.t[:], wb1[:, sl * 1024 + k * 128: sl * 1024 + (k + 1) * 128],
                            xlnT[:, k * 2048 + c * 512: k * 2048 + (c + 1) * 512],
                            start=(k == 0), stop=(k == NF - 1)),
                            ([t_wl, bk.r] + t_trf) if k == 0 else [])
                    bk.w = mm
                    t_w_rd[sl] = mm
                    ev = TOK("scalar", lambda e, fi=fi, bk=bk: e.copy(
                        vT[:, fi * 512:(fi + 1) * 512], bk.t[:]),
                        [bk.w] + ([t_buf_rd] if fi == 0 else []))
                    bk.r = ev
                    t_vT[fi] = ev

                def tp(srcbuf, dstbuf, deps_all):
                    evs = []
                    dfull = dstbuf.rearrange("p (st fj q) -> p st fj q", st=4, fj=8)
                    pin = pbT_t[:].rearrange("p (j st q) -> p st j q", j=2, st=4)
                    for f0 in range(0, NF, 2):
                        mm = None
                        for bi in range(8):
                            fi2 = f0 + bi // 4
                            st = bi % 4
                            mm = TOK("tensor", lambda e, fi2=fi2, st=st, bi=bi, srcbuf=srcbuf:
                                     e.transpose(
                                         pbT_t[:, bi * 128:(bi + 1) * 128],
                                         srcbuf[:, fi2 * 512 + st * 128: fi2 * 512 + (st + 1) * 128],
                                         identb[:]),
                                     ([bT.r] + deps_all) if bi == 0 else [])
                        bT.w = mm
                        dap = dfull[:, :, f0:f0 + 2, :]
                        ev = TOK("scalar", lambda e, dap=dap, pin=pin: e.copy(dap, pin), [bT.w])
                        bT.r = ev
                        evs.append(ev)
                    return evs
                evk = tp(akT, akt, t_akT)
                evv = tp(vT, av, t_vT)
                t_kv = p.fence("scalar")
                for h in range(HEADS):
                    pi = h % 2
                    fi = h // 2
                    ro = 64 * pi
                    mmA = None
                    for s in range(NST):
                        mmA = TOK("tensor", lambda e, s=s, fi=fi, ro=ro: e.matmul(
                            bA.t[:, s * 128:(s + 1) * 128],
                            akT[ro:ro + 64, fi * 512 + s * 128: fi * 512 + (s + 1) * 128],
                            aqs[ro:ro + 64, fi * 512 + s * 128: fi * 512 + (s + 1) * 128],
                            start=True, stop=True, skip_group_check=True),
                            ([bA.r, t_aqs[fi], t_kv] if s == 0 else []))
                    bA.w = mmA
                    am = amask[:, pi * 512:(pi + 1) * 512]
                    tA = TOK("vector", lambda e, am=am, bA=bA: e.tensor_mul(
                        am, bA.t[:], trimask[:]), [bA.w, tok_consts, t_amask_rd[pi]])
                    bA.r = tA
                    sb = bS[h // 8]
                    scol = (h % 8) * 64
                    ssl = ssb[ro:ro + 64, pi * 64:(pi + 1) * 64]
                    yb = bY[pi]
                    for s in range(NST):
                        first = (c == 0 and s == 0)
                        if not first:
                            cp = TOK("scalar", lambda e, sb=sb, scol=scol, ro=ro, ssl=ssl: e.copy(
                                ssl, sb.t[ro:ro + 64, scol:scol + 64]),
                                [tok_S_w[h], t_ssb_rd[pi]])
                            tok_S_r[h] = cp
                        st_tok = TOK("tensor",
                                     lambda e, s=s, h=h, sb=sb, scol=scol, ro=ro,
                                     st0=(not s_started[h // 8]): e.matmul(
                                         sb.t[ro:ro + 64, scol:scol + 64],
                                         akt[:, s * 1024 + h * 64: s * 1024 + (h + 1) * 64],
                                         av[:, s * 1024 + h * 64: s * 1024 + (h + 1) * 64],
                                         start=st0, stop=False, skip_group_check=True),
                                     [tok_S_r[h], t_kv] if not first else [sb.r, t_kv])
                        s_started[h // 8] = True
                        tok_S_w[h] = st_tok
                        sb.w = st_tok
                        dd = [tA, yb.r] if s == 0 else [tA]
                        y1 = TOK("tensor", lambda e, s=s, h=h, yb=yb, ro=ro, am=am, first=first:
                                 e.matmul(yb.t[ro:ro + 64, s * 128:(s + 1) * 128],
                                          av[:, s * 1024 + h * 64: s * 1024 + (h + 1) * 64],
                                          am[:, s * 128:(s + 1) * 128],
                                          start=True, stop=first, skip_group_check=True), dd)
                        yb.w = y1
                        if not first:
                            y2 = TOK("tensor", lambda e, s=s, fi=fi, yb=yb, ro=ro, ssl=ssl:
                                     e.matmul(yb.t[ro:ro + 64, s * 128:(s + 1) * 128],
                                              ssl,
                                              aqs[ro:ro + 64, fi * 512 + s * 128:
                                                  fi * 512 + (s + 1) * 128],
                                              start=False, stop=True, skip_group_check=True),
                                     [tok_S_r[h]])
                            yb.w = y2
                    ye = TOK("scalar", lambda e, yb=yb, ro=ro, fi=fi: e.copy(
                        ayT[ro:ro + 64, fi * 512:(fi + 1) * 512], yb.t[ro:ro + 64, :]),
                        [yb.w, t_ayT_rd])
                    yb.r = ye
                    t_amask_rd[pi] = p.fence("tensor")
                    t_ssb_rd[pi] = p.fence("tensor")
                t_Y_all = p.fence("scalar")
                for oh in range(2):
                    t_wo = [None] * NF
                    last_mm = [None] * NST
                    for k in range(NF):
                        sl = k % 2
                        t_wo[k] = DMA(wvo[:, sl * 512:(sl + 1) * 512], wo_h[oh, k],
                                      [t_wo_rd[sl]], q="gpsimd")
                        for st in range(NST):
                            bk = bDel[st]
                            dd = []
                            if k == 0:
                                dd = [t_wo[k], bk.r, t_Y_all]
                            elif st == 0:
                                dd = [t_wo[k]]
                            last_mm[st] = TOK("tensor", lambda e, k=k, st=st, sl=sl, bk=bk:
                                e.matmul(bk.t[:],
                                         ayT[:, k * 512 + st * 128: k * 512 + (st + 1) * 128],
                                         wvo[:, sl * 512:(sl + 1) * 512],
                                         start=(k == 0), stop=(k == NF - 1),
                                         skip_group_check=True), dd)
                        t_wo_rd[sl] = last_mm[NST - 1]
                    for st in range(NST):
                        bk = bDel[st]
                        bk.w = last_mm[st]
                        tt = 4 * c + st
                        colx = tt * 1024 + oh * 512
                        ra = TOK("vector", lambda e, colx=colx, bk=bk: e.tensor_add(
                            xres[:, colx:colx + 512], xres[:, colx:colx + 512], bk.t[:]),
                            [bk.w, tok_x[tt]])
                        bk.r = ra
                        if oh == 1:
                            tok_x[tt] = ra
                t_ayT_rd = p.fence("tensor")
                t_buf_rd = p.fence("tensor")

        for ph in phases:
            if ph == "F":
                ffn_sublayer()
            elif ph == "A":
                attn_sublayer()
            elif ph == "L":
                ffn_sublayer(upto="ln")
            elif ph == "N":
                ffn_sublayer(upto="ln", no_tr=True)
            elif ph == "C":
                ffn_sublayer(upto="cnt")

        st_toks = []
        for i in range(4):
            dst = out_h.rearrange("(t p) f -> p t f", p=128)[:, 4 * i:4 * i + 4, :]
            src = xres[:, 4 * i * 1024:(4 * i + 4) * 1024].rearrange("p (t f) -> p t f", f=1024)
            st_toks.append(DMA(dst, src, [tok_x[4 * i + j] for j in range(4)]))

        if dump is not None:
            bufmap = {"xlnb": (xlnb, BF16), "xlnT": (xlnT, BF16), "hT": (hT, BF16),
                      "scr": (scr, F32), "scor": (scor, F32), "hfm": (hfm, F32),
                      "csg": (csg, F32), "xres": (xres, F32), "mus": (mus, F32),
                      "rsd": (rsd, F32)}
            buf, dt = bufmap[dump]
            bar = p.barrier()
            if dt == F32:
                n = {"scr": 2048, "scor": 16, "hfm": 16, "csg": 16, "xres": 16384,
                     "mus": 16, "rsd": 16}[dump]
                DMA(dbg_h[:, 0:n], buf[:, 0:n], bar)
            else:
                nswept = 16384
                toks = bar
                for j0 in range(0, nswept, 2048):
                    cv = TOK("vector", lambda e, j0=j0: e.tensor_copy(
                        scr[:], buf[:, j0:j0 + 2048]), toks)
                    dtok = DMA(dbg_h[:, j0:j0 + 2048], scr[:], [cv])
                    toks = [dtok]

        final_sync = p.cnt["sync"]

        def emit(e, name):
            hi = {k: 0 for k in ENGINES}
            for w, fn, inc in p.steps[name]:
                for de, dv in w.items():
                    if dv > hi[de]:
                        e.wait_ge(sems[de], dv)
                        hi[de] = dv
                ins = fn(e)
                if name == "vector":
                    ins = e.drain()
                ins.then_inc(sems[name], inc)

        @block.sync
        def _(e):
            emit(e, "sync")

        @block.tensor
        def _(e):
            emit(e, "tensor")

        @block.vector
        def _(e):
            emit(e, "vector")

        @block.scalar
        def _(e):
            emit(e, "scalar")

        @block.gpsimd
        def _(e):
            emit(e, "gpsimd")
            e.wait_ge(sems["sync"], final_sync)

    mybir.codegen_inst_isa_subclasses(nc)
    return nc


def prep_inputs(x_b, w, w_out, w_r, b_r, W1, b1, W2, b2):
    f32 = np.float32
    bf16 = ml_dtypes.bfloat16
    wT = np.ascontiguousarray(w.T)
    wqk = np.ascontiguousarray(
        wT[:, :2048].reshape(8, 128, 16, 128).transpose(2, 1, 0, 3).reshape(16, 128, 1024))
    wvm = np.ascontiguousarray(
        wT[:, 2048:].reshape(8, 128, 8, 128).transpose(2, 1, 0, 3).reshape(8, 128, 1024))
    wo = np.ascontiguousarray(
        np.asarray(w_out).reshape(8, 128, 2, 512).transpose(2, 0, 1, 3))
    w1 = np.ascontiguousarray(
        np.asarray(W1).reshape(8, 128, 32, 128).transpose(2, 1, 0, 3).reshape(32, 128, 1024))
    w2 = np.ascontiguousarray(
        np.asarray(W2).reshape(32, 128, 2, 512).transpose(2, 0, 1, 3))
    b1c = np.ascontiguousarray(np.asarray(b1).reshape(32, 128).T)
    b2r = np.ascontiguousarray(np.asarray(b2).reshape(1, 1024))
    wrb = np.ascontiguousarray(np.broadcast_to(np.asarray(w_r).reshape(1, 1024), (128, 1024)))
    brb = np.ascontiguousarray(np.broadcast_to(np.asarray(b_r).reshape(1, 1), (128, 1)))
    identb = np.eye(128, dtype=f32).astype(bf16)
    tri = np.triu(np.ones((128, 128), f32))
    trimask = np.ascontiguousarray(np.tile(tri, (1, 4))).astype(bf16)
    ones1f = np.ones((1, 128), f32)
    ones2 = np.ones((2, 128), f32).astype(bf16)
    ind2 = np.zeros((128, 2), f32)
    ind2[0:64, 0] = 1.0
    ind2[64:128, 1] = 1.0
    return {
        "x": np.ascontiguousarray(np.asarray(x_b), f32), "wqk": wqk, "wvm": wvm,
        "wo": wo, "w1": w1, "w2": w2,
        "b1c": b1c, "b2r": b2r, "wrb": wrb, "brb": brb,
        "identb": identb, "trimask": trimask, "ones1f": ones1f, "ones2": ones2,
        "ind2": ind2.astype(bf16), "ind2T": np.ascontiguousarray(ind2.T).astype(bf16),
        "onesK1": np.ones((1, 128), f32).astype(bf16),
    }


_NC_CACHE = {}


def kernel(L0_attn_x, L0_attn_w, L0_attn_w_out, L0_ffn_w_r, L0_ffn_b_r,
           L0_ffn_W1, L0_ffn_b1, L0_ffn_W2, L0_ffn_b2):
    x = np.asarray(L0_attn_x, dtype=np.float32)
    nc = build(phases="FAF")
    shared = None
    in_maps = []
    for b in range(8):
        im = prep_inputs(x[b], np.asarray(L0_attn_w), np.asarray(L0_attn_w_out),
                         np.asarray(L0_ffn_w_r), np.asarray(L0_ffn_b_r),
                         np.asarray(L0_ffn_W1), np.asarray(L0_ffn_b1),
                         np.asarray(L0_ffn_W2), np.asarray(L0_ffn_b2)) if b == 0 else dict(shared)
        if b == 0:
            shared = im
        im = dict(im)
        im["x"] = np.ascontiguousarray(x[b], np.float32)
        in_maps.append(im)
    res = run_bass_kernel_spmd(nc, in_maps, core_ids=list(range(8)))
    out = np.stack([np.asarray(res.results[b]["out"]) for b in range(8)], axis=0)
    return out.astype(np.float32)


# version 19
# speedup vs baseline: 1.1971x; 1.1971x over previous
import sys
sys.path.insert(0, "/opt/trn_rl_repo")
import numpy as np
import ml_dtypes
import concourse.bass as bass
import concourse.mybir as mybir
from concourse.bass_utils import run_bass_kernel_spmd

F32 = mybir.dt.float32
BF16 = mybir.dt.bfloat16
AF = mybir.ActivationFunctionType
OP = mybir.AluOpType
AX = mybir.AxisListType

T, D, HEADS, L, DH = 2048, 1024, 16, 64, 4096
NT, NF, NH = 16, 8, 32
CH, NCH, NST = 512, 4, 4
SCALE = 0.125

ENGINES = ("sync", "gpsimd", "tensor", "vector", "scalar")


class Prog:
    def __init__(self):
        self.steps = {e: [] for e in ENGINES}
        self.cnt = {e: 0 for e in ENGINES}

    def on(self, eng, fn, deps=(), inc=1):
        w = {}
        for d in deps:
            if d is None:
                continue
            de, dv = d
            if de == eng:
                continue
            w[de] = max(w.get(de, 0), dv)
        self.cnt[eng] += inc
        self.steps[eng].append((w, fn, inc))
        return (eng, self.cnt[eng])

    def fence(self, eng):
        return (eng, self.cnt[eng])

    def barrier(self):
        return [self.fence(e) for e in ENGINES]


class Bank:
    def __init__(self, t):
        self.t = t
        self.w = None
        self.r = None


def build(phases="FAF", dump=None):
    nc = bass.Bass("TRN2", target_bir_lowering=False)

    def din(name, shape, dtype=F32):
        return nc.dram_tensor(name, shape, dtype, kind="ExternalInput").ap()

    x_h = din("x", [T, D])
    wqk_h = din("wqk", [16, 128, 1024])
    wvm_h = din("wvm", [8, 128, 1024])
    wo_h = din("wo", [2, 8, 128, 512])
    w1_h = din("w1", [32, 128, 1024])
    w2_h = din("w2", [2, 32, 128, 512])
    b1c_h = din("b1c", [128, 32])
    b2r_h = din("b2r", [1, 1024])
    wrb_h = din("wrb", [128, 1024])
    brb_h = din("brb", [128, 1])
    identb_h = din("identb", [128, 128], BF16)
    trimask_h = din("trimask", [128, 512], BF16)
    ones1f_h = din("ones1f", [1, 128])
    ones2_h = din("ones2", [2, 128], BF16)
    ind2T_h = din("ind2T", [2, 128], BF16)
    ind2_h = din("ind2", [128, 2], BF16)
    onesK1_h = din("onesK1", [1, 128], BF16)

    w1s_h = nc.dram_tensor("w1s", [32, 128, 1024], BF16, kind="Internal").ap()
    w2s_h = nc.dram_tensor("w2s", [2, 32, 128, 512], BF16, kind="Internal").ap()
    wqks_h = nc.dram_tensor("wqks", [16, 128, 1024], BF16, kind="Internal").ap()
    wvms_h = nc.dram_tensor("wvms", [8, 128, 1024], BF16, kind="Internal").ap()
    wos_h = nc.dram_tensor("wos", [2, 8, 128, 512], BF16, kind="Internal").ap()
    out_h = nc.dram_tensor("out", [T, D], F32, kind="ExternalOutput").ap()
    dbg_h = None
    if dump is not None:
        dbg_h = nc.dram_tensor("dbg", [128, 16384], F32, kind="ExternalOutput").ap()

    from contextlib import ExitStack
    with ExitStack() as ctx:
        E = ctx.enter_context
        xres = E(nc.sbuf_tensor("xres", [128, NT * 1024], F32))
        xlnT = E(nc.sbuf_tensor("xlnT", [128, NF * 2048], BF16))
        xlnb = E(nc.sbuf_tensor("xlnb", [128, 16384], BF16))
        hT = E(nc.sbuf_tensor("hT", [128, 16384], BF16))
        wb1 = E(nc.sbuf_tensor("wb1", [128, 2048], BF16))
        stg = E(nc.sbuf_tensor("stg", [128, 4096], BF16))
        wb2 = E(nc.sbuf_tensor("wb2", [128, 1024], BF16))
        wvo = E(nc.sbuf_tensor("wvo", [128, 1024], BF16))
        scr = E(nc.sbuf_tensor("scr", [128, 2048], F32))
        sden = E(nc.sbuf_tensor("sden", [2, 512], BF16))
        ssb = E(nc.sbuf_tensor("ssb", [128, 128], BF16))
        amask = E(nc.sbuf_tensor("amask", [128, 1024], BF16))
        srow = E(nc.sbuf_tensor("srow", [1, 2048], F32))
        stats = E(nc.sbuf_tensor("stats", [128, 12], F32))
        smv = E(nc.sbuf_tensor("smv", [128, 2], F32))
        sstd = E(nc.sbuf_tensor("sstd", [128, 1], F32))
        seps = E(nc.sbuf_tensor("seps", [128, 1], F32))
        mus = E(nc.sbuf_tensor("mus", [128, 16], F32))
        rsd = E(nc.sbuf_tensor("rsd", [128, 16], F32))
        scor = E(nc.sbuf_tensor("scor", [128, 16], F32))
        nsc = E(nc.sbuf_tensor("nsc", [128, 16], F32))
        csg = E(nc.sbuf_tensor("csg", [128, 16], F32))
        hfm = E(nc.sbuf_tensor("hfm", [128, 16], F32))
        wbar = E(nc.sbuf_tensor("wbar", [128, 1], F32))
        stmp = E(nc.sbuf_tensor("stmp", [128, 1], F32))
        stmp6 = E(nc.sbuf_tensor("stmp6", [128, 1], F32))
        sdot = E(nc.sbuf_tensor("sdot", [128, 1], F32))
        sacc = E(nc.sbuf_tensor("sacc", [128, 1], F32))
        carry = E(nc.sbuf_tensor("carry", [128, 8], F32))
        identb = E(nc.sbuf_tensor("identb_s", [128, 128], BF16))
        trimask = E(nc.sbuf_tensor("trimask_s", [128, 512], BF16))
        ones1f = E(nc.sbuf_tensor("ones1f_s", [1, 128], F32))
        ones2 = E(nc.sbuf_tensor("ones2_s", [2, 128], BF16))
        ind2T = E(nc.sbuf_tensor("ind2T_s", [2, 128], BF16))
        ind2 = E(nc.sbuf_tensor("ind2_s", [128, 2], BF16))
        onesK1 = E(nc.sbuf_tensor("onesK1_s", [1, 128], BF16))
        b1c = E(nc.sbuf_tensor("b1c_s", [128, 32], F32))
        b2rb = E(nc.sbuf_tensor("b2rb_s", [1, 1024], BF16))
        wrb = E(nc.sbuf_tensor("wrb_s", [128, 1024], F32))
        brb = E(nc.sbuf_tensor("brb_s", [128, 1], F32))

        pbT_t = E(nc.psum_tensor("pbT", [128, 1024], BF16))
        pb = [E(nc.psum_tensor(f"pb{i}", [128, 512], F32)) for i in range(7)]
        sems = {e: E(nc.semaphore(f"sem_{e}")) for e in ENGINES}
        block = E(nc.Block())

        p = Prog()
        bT = Bank(pbT_t)
        b = [Bank(t) for t in pb]
        bEv = [b[0], b[1]]
        bA = b[2]
        bY = [b[3], b[4]]
        bS = [b[5], b[6]]
        bF2 = [b[2], b[3], b[4], b[5]]
        bDel = [b[0], b[1], b[2], b[3]]

        av = hT[:, 0:4096]
        akt = hT[:, 4096:8192]
        akT = hT[:, 8192:12288]
        aqs = hT[:, 12288:16384]
        ayT = xlnb[:, 0:4096]
        vT = xlnb[:, 4096:8192]
        alns = xlnb[:, 8192:10240]

        tok_x = [None] * NT

        def DMA(out, in_, deps=(), q="sync"):
            return p.on(q, lambda e, o=out, i=in_: e.dma_start(out=o, in_=i), deps, inc=16)

        def TOK(eng, fn, deps=()):
            return p.on(eng, fn, deps)

        for i in range(4):
            src = x_h.rearrange("(t p) f -> p t f", p=128)[:, 4 * i:4 * i + 4, :]
            dst = xres[:, 4 * i * 1024:(4 * i + 4) * 1024].rearrange("p (t f) -> p t f", f=1024)
            tk = DMA(dst, src)
            for j in range(4):
                tok_x[4 * i + j] = tk
        for dst, srch in ((identb, identb_h), (trimask, trimask_h), (ones1f, ones1f_h),
                          (ones2, ones2_h), (ind2, ind2_h), (onesK1, onesK1_h), (ind2T, ind2T_h),
                          (b1c, b1c_h), (b2rb, b2r_h), (wrb, wrb_h), (brb, brb_h)):
            DMA(dst[:], srch, q="gpsimd")
        tok_consts = p.fence("gpsimd")
        t_staged = {}
        stg_slot_rd = [None] * 4
        def _stage(name, pairs):
            si = [0]
            for srcb, dstb, wdt in pairs:
                sl = si[0] % 4
                si[0] += 1
                t1 = DMA(stg[:, sl * 1024: sl * 1024 + wdt], srcb,
                         [stg_slot_rd[sl]], q="gpsimd")
                t2 = DMA(dstb, stg[:, sl * 1024: sl * 1024 + wdt], [t1], q="sync")
                stg_slot_rd[sl] = t2
            t_staged[name] = p.fence("sync")
        _stage("w1", [(w1_h[m], w1s_h[m], 1024) for m in range(32)])
        _stage("wqk", [(wqk_h[m], wqks_h[m], 1024) for m in range(16)])
        _stage("w2", [(w2_h[oc, k], w2s_h[oc, k], 512) for oc in range(2) for k in range(32)])
        _stage("wvm", [(wvm_h[m], wvms_h[m], 1024) for m in range(8)])
        _stage("wo", [(wo_h[oc, k], wos_h[oc, k], 512) for oc in range(2) for k in range(8)])
        t_eps = TOK("vector", lambda e: e.memset(seps[:], 1e-5))
        t_eps6 = TOK("vector", lambda e: e.memset(stmp6[:], 1e-6))
        def _wbar(e):
            e.tensor_reduce(wbar[:], wrb[:], AX.X, OP.add)
            return e.drain()
        t_wbar = TOK("vector", _wbar, [tok_consts])

        xlnT_f = xlnT[:].rearrange("p (f t) -> p f t", f=NF)
        pbT_f = pbT_t[:].rearrange("p (f q) -> p f q", f=8)
        pbT_f3 = lambda: pbT_t[:].rearrange("p (f q) -> p f q", f=8)

        def ffn_sublayer(upto="all", no_tr=False):
            bar = p.barrier()
            t_norm = [None] * NT
            t_tr = [None] * NT
            prev_s4 = None
            for tt in range(NT):
                xt = xres[:, tt * 1024:(tt + 1) * 1024]
                dst = xlnb[:, tt * 1024:(tt + 1) * 1024]
                d = [tok_x[tt]] + (bar if tt == 0 else [])
                s1 = TOK("vector", lambda e, xt=xt: e.bn_stats(stats[:, 0:6], xt[:, 0:512]), d)
                def _bns2(e, xt=xt):
                    e.bn_stats(stats[:, 6:12], xt[:, 512:1024])
                    return e.drain()
                TOK("vector", _bns2)
                def _aggr(e):
                    e.bn_aggr(smv[:], stats[:])
                    return e.drain()
                s3 = TOK("vector", _aggr, [prev_s4])
                s4 = TOK("scalar", lambda e: e.activation(sstd[:], smv[:, 1:2], AF.Sqrt,
                                                          bias=seps[:], scale=1.0),
                         [s3, t_eps] + (bar if tt == 0 else []))
                prev_s4 = s4
                def _ttr(e, xt=xt, tt=tt):
                    e.tensor_tensor_reduce(
                        out=scr[:, 0:1024], in0=xt, in1=wrb[:], scale=1.0, scalar=0.0,
                        op0=OP.mult, op1=OP.add, accum_out=sdot[:])
                    e.drain()
                    return e.tensor_copy(scor[:, tt:tt + 1], sdot[:])
                s8 = TOK("vector", _ttr, [tok_consts])
                def _recip(e, tt=tt):
                    e.reciprocal(sdot[:], sstd[:])
                    e.drain()
                    return e.tensor_copy(rsd[:, tt:tt + 1], sdot[:])
                s5 = TOK("vector", _recip, [s4])
                s6 = TOK("vector", lambda e, tt=tt: e.tensor_copy(mus[:, tt:tt + 1], smv[:, 0:1]))
                t_norm[tt] = TOK("vector", lambda e, xt=xt, dst=dst, tt=tt: e.tensor_scalar(
                    dst, xt, mus[:, tt:tt + 1], rsd[:, tt:tt + 1], OP.subtract, OP.mult))
                s9 = TOK("vector", lambda e, tt=tt: e.tensor_scalar(
                    stmp[:], mus[:, tt:tt + 1], wbar[:], None, OP.mult), [t_wbar])
                s10 = TOK("vector", lambda e, tt=tt: e.tensor_sub(
                    stmp[:], scor[:, tt:tt + 1], stmp[:]))
                s11 = TOK("vector", lambda e, tt=tt: e.tensor_scalar(
                    scor[:, tt:tt + 1], stmp[:], rsd[:, tt:tt + 1], brb[:], OP.mult, OP.add))
                TOK("vector", lambda e, tt=tt: e.tensor_scalar(
                    nsc[:, tt:tt + 1], scor[:, tt:tt + 1], -1.0, None, OP.mult))
                if no_tr:
                    t_tr[tt] = t_norm[tt]
                    continue
                mm = None
                for bi in range(NF):
                    mm = TOK("tensor", lambda e, dst=dst, bi=bi: e.transpose(
                        pbT_t[:, bi * 128:(bi + 1) * 128], dst[:, bi * 128:(bi + 1) * 128], identb[:]),
                        ([bT.r, tok_consts, t_norm[tt]] + (bar if tt == 0 else []))
                        if bi == 0 else [])
                bT.w = mm
                ev = None
                for bi in range(NF):
                    ev = TOK("scalar", lambda e, bi=bi, tt=tt: e.copy(
                        xlnT[:, bi * 2048 + tt * 128: bi * 2048 + (tt + 1) * 128],
                        pbT_t[:, bi * 128:(bi + 1) * 128]), [bT.w] if bi == 0 else [])
                bT.r = ev
                t_tr[tt] = ev

            if upto == "ln":
                return
            t_sc = p.fence("vector")
            t_row = DMA(srow[:], scor[:], [t_sc], q="gpsimd")
            t_bc = []
            for j in range(4):
                bk = bEv[j % 2]
                mmt = TOK("tensor", lambda e, j=j, bk=bk: e.matmul(
                    bk.t[:], ones1f[:], srow[:, j * 512:(j + 1) * 512], start=True, stop=True),
                    [t_row, tok_consts, bk.r])
                bk.w = mmt
                ev = TOK("scalar", lambda e, j=j, bk=bk: e.copy(
                    scr[:, j * 512:(j + 1) * 512], bk.t[:]), [bk.w, t_sc])
                bk.r = ev
                t_bc.append(ev)
            t_cnt = []
            for tt in range(NT):
                def _sgn(e, tt=tt):
                    e.activation(hT[:, 0:2048], scr[:, 0:2048], AF.Sign,
                                 bias=nsc[:, tt:tt + 1], scale=1.0, accum_out=sacc[:])
                    e.drain()
                    return e.copy(csg[:, tt:tt + 1], sacc[:])
                t_cnt.append(TOK("scalar", _sgn,
                                 t_bc + ([p.fence("vector")] if tt == 0 else [])))
            t_hfm = TOK("vector", lambda e: e.tensor_scalar(
                hfm[:], csg[:], -2.5, 0.5, OP.is_le, OP.mult), t_cnt)

            if upto == "cnt":
                return
            tok_hT_rd = [t_hfm] * NH
            t_w1_rd = [None, None]
            t_w2_rd = [None, None]
            t_epi_scr = None
            for c in range(NCH):
                t_trf = [t_tr[4 * c + j] for j in range(4)]
                tok_hT = [None] * NH
                for m in range(NH):
                    sl = m % 2
                    t_wl = DMA(wb1[:, sl * 1024:(sl + 1) * 1024], w1s_h[m],
                               [t_w1_rd[sl], t_staged["w1"]], q="sync")
                    bk = bEv[m % 2]
                    mm = None
                    for k in range(NF):
                        mm = TOK("tensor", lambda e, k=k, sl=sl, c=c, bk=bk: e.matmul(
                            bk.t[:], wb1[:, sl * 1024 + k * 128: sl * 1024 + (k + 1) * 128],
                            xlnT[:, k * 2048 + c * 512: k * 2048 + (c + 1) * 512],
                            start=(k == 0), stop=(k == NF - 1)),
                            ([t_wl, bk.r] + t_trf) if k == 0 else [])
                    bk.w = mm
                    t_w1_rd[sl] = mm
                    ge = TOK("scalar", lambda e, m=m, bk=bk: e.activation(
                        hT[:, m * 512:(m + 1) * 512], bk.t[:], AF.Gelu,
                        bias=b1c[:, m:m + 1], scale=1.0),
                        [bk.w, tok_consts, tok_hT_rd[m]])
                    bk.r = ge
                    tok_hT[m] = ge
                tok_hT_rd = [None] * NH
                for oc in range(2):
                    last_mm = [None] * NST
                    for k2 in range(NH):
                        sl = k2 % 2
                        t_w2 = DMA(wb2[:, sl * 512:(sl + 1) * 512], w2s_h[oc, k2],
                                   [t_w2_rd[sl], t_staged["w2"]], q="sync")
                        for m2 in range(NST):
                            bk = bF2[m2]
                            dd = []
                            if k2 == 0:
                                dd = [t_w2, bk.r, tok_hT[k2]]
                            elif m2 == 0:
                                dd = [t_w2, tok_hT[k2]]
                            last_mm[m2] = TOK("tensor", lambda e, k2=k2, m2=m2, sl=sl, bk=bk:
                                e.matmul(bk.t[:],
                                         hT[:, k2 * 512 + m2 * 128: k2 * 512 + (m2 + 1) * 128],
                                         wb2[:, sl * 512:(sl + 1) * 512],
                                         start=(k2 == 0), stop=False,
                                         skip_group_check=True), dd)
                        t_w2_rd[sl] = last_mm[NST - 1]
                        tok_hT_rd[k2] = last_mm[NST - 1]
                    for m2 in range(NST):
                        bk = bF2[m2]
                        bk.w = TOK("tensor", lambda e, oc=oc, bk=bk: e.matmul(
                            bk.t[:], onesK1[:], b2rb[:, oc * 512:(oc + 1) * 512],
                            start=False, stop=True, skip_group_check=True), [tok_consts])
                    for m2 in range(NST):
                        bk = bF2[m2]
                        tt = 4 * c + m2
                        colx = tt * 1024 + oc * 512
                        xh = xres[:, colx:colx + 512]
                        lh = xlnb[:, colx:colx + 512]
                        d1 = TOK("vector", lambda e, bk=bk, lh=lh: e.tensor_sub(
                            scr[:, 0:512], bk.t[:], lh), [bk.w, t_hfm, t_epi_scr])
                        bk.r = d1
                        TOK("vector", lambda e, xh=xh, lh=lh: e.scalar_tensor_tensor(
                            xh, lh, 0.5, xh, OP.mult, OP.add))
                        d3 = TOK("vector", lambda e, xh=xh, tt=tt: e.scalar_tensor_tensor(
                            xh, scr[:, 0:512], hfm[:, tt:tt + 1], xh, OP.mult, OP.add))
                        t_epi_scr = d3
                        if oc == 1:
                            tok_x[tt] = d3

        def attn_sublayer():
            bar = p.barrier()
            t_tr = [None] * NT
            prev_s4a = None
            for tt in range(NT):
                xt = xres[:, tt * 1024:(tt + 1) * 1024]
                sl = tt % 2
                dst = alns[:, sl * 1024:(sl + 1) * 1024]
                d = [tok_x[tt]] + (bar if tt == 0 else [])
                TOK("vector", lambda e, xt=xt: e.bn_stats(stats[:, 0:6], xt[:, 0:512]), d)
                def _bns2a(e, xt=xt):
                    e.bn_stats(stats[:, 6:12], xt[:, 512:1024])
                    return e.drain()
                TOK("vector", _bns2a)
                def _aggra(e):
                    e.bn_aggr(smv[:], stats[:])
                    return e.drain()
                s3 = TOK("vector", _aggra, [prev_s4a])
                s4 = TOK("scalar", lambda e: e.activation(sstd[:], smv[:, 1:2], AF.Sqrt,
                                                          bias=seps[:], scale=1.0),
                         [s3, t_eps] + (bar if tt == 0 else []))
                prev_s4a = s4
                def _recipa(e):
                    e.reciprocal(sstd[:], sstd[:])
                    return e.drain()
                s5 = TOK("vector", _recipa, [s4])
                wdep = [t_tr[tt - 2]] if tt >= 2 else []
                t_norm = TOK("vector", lambda e, xt=xt, dst=dst: e.tensor_scalar(
                    dst, xt, smv[:, 0:1], sstd[:], OP.subtract, OP.mult), wdep)
                mm = None
                for bi in range(NF):
                    mm = TOK("tensor", lambda e, dst=dst, bi=bi: e.transpose(
                        pbT_t[:, bi * 128:(bi + 1) * 128], dst[:, bi * 128:(bi + 1) * 128], identb[:]),
                        ([bT.r, tok_consts, t_norm] + (bar if tt == 0 else []))
                        if bi == 0 else [])
                bT.w = mm
                ev = None
                for bi in range(NF):
                    ev = TOK("scalar", lambda e, bi=bi, tt=tt: e.copy(
                        xlnT[:, bi * 2048 + tt * 128: bi * 2048 + (tt + 1) * 128],
                        pbT_t[:, bi * 128:(bi + 1) * 128]), [bT.w] if bi == 0 else [])
                bT.r = ev
                t_tr[tt] = ev

            t_zc = TOK("vector", lambda e: e.memset(carry[:], 0.0), bar)
            s_started = [False, False]
            tok_S_w = [None] * HEADS
            tok_S_r = [None] * HEADS
            t_w_rd = [None, None]
            t_w2_rd = [None, None]
            t_wo_rd = [None, None]
            t_amask_rd = [None, None]
            t_ssb_rd = [None, None]
            t_ayT_rd = None
            t_buf_rd = None
            prev_sd = None
            prev_p3 = None
            prev_qs = None

            for c in range(NCH):
                t_trf = [t_tr[4 * c + j] for j in range(4)]
                t_aqs = [None] * NF
                t_akT = [None] * NF
                for fi in range(NF):
                    mmk = mmq = None
                    for (mi, which) in ((8 + fi, "k"), (fi, "q")):
                        sl = mi % 2
                        t_wl = DMA(wb1[:, sl * 1024:(sl + 1) * 1024], wqks_h[mi],
                                   [t_w_rd[sl], t_staged["wqk"]], q="sync")
                        bk = bEv[0] if which == "k" else bEv[1]
                        mm = None
                        for k in range(NF):
                            mm = TOK("tensor", lambda e, k=k, sl=sl, c=c, bk=bk: e.matmul(
                                bk.t[:], wb1[:, sl * 1024 + k * 128: sl * 1024 + (k + 1) * 128],
                                xlnT[:, k * 2048 + c * 512: k * 2048 + (c + 1) * 512],
                                start=(k == 0), stop=(k == NF - 1)),
                                ([t_wl, bk.r] + t_trf) if k == 0 else [])
                        bk.w = mm
                        t_w_rd[sl] = mm
                        if which == "k":
                            mmk = mm
                        else:
                            mmq = mm
                    bkk, bkq = bEv[0], bEv[1]
                    e1 = TOK("scalar", lambda e, bkk=bkk: e.activation(
                        scr[:, 0:512], bkk.t[:], AF.Exp, bias=0.0, scale=SCALE),
                        [bkk.w, prev_sd])
                    e2 = TOK("scalar", lambda e: e.activation(
                        scr[:, 0:512], scr[:, 0:512], AF.Identity,
                        bias=stmp6[:], scale=1.0), [e1, t_eps6])
                    bkk.r = e2
                    t_akT[fi] = TOK("vector", lambda e, fi=fi: e.tensor_copy(
                        akT[:, fi * 512:(fi + 1) * 512], scr[:, 0:512]),
                        [e2] + ([t_buf_rd] if fi == 0 else []))
                    TOK("vector", lambda e, fi=fi: e.tensor_tensor_scan(
                        scr[:, 512:1024], scr[:, 0:512], scr[:, 0:512], carry[:, fi:fi + 1],
                        OP.add, OP.bypass), [t_zc])
                    sd = TOK("vector", lambda e: e.drain())
                    prev_sd = sd
                    TOK("vector", lambda e, fi=fi: e.tensor_copy(
                        carry[:, fi:fi + 1], scr[:, 1023:1024]))
                    q1 = TOK("scalar", lambda e, bkq=bkq: e.copy(scr[:, 1536:2048], bkq.t[:]),
                             [bkq.w, prev_p3])
                    q2 = TOK("scalar", lambda e, fi=fi, bkq=bkq: e.activation(
                        aqs[:, fi * 512:(fi + 1) * 512], bkq.t[:], AF.Exp, bias=0.0, scale=SCALE),
                        [bkq.w] + ([t_buf_rd] if fi == 0 else []))
                    bkq.r = q2
                    mmd = TOK("tensor", lambda e, fi=fi, bkk=bkk: e.matmul(
                        bkk.t[0:2, :], ind2[:], aqs[:, fi * 512:(fi + 1) * 512],
                        start=True, stop=True, skip_group_check=True),
                        [q2, bkk.r, tok_consts])
                    bkk.w = mmd
                    de = TOK("scalar", lambda e, bkk=bkk: e.copy(sden[:], bkk.t[0:2, :]), [mmd])
                    bkk.r = de
                    mmb = TOK("tensor", lambda e, bkq=bkq: e.matmul(
                        bkq.t[:], ind2T[:], sden[:], start=True, stop=True,
                        skip_group_check=True), [de, bkq.r, tok_consts])
                    bkq.w = mmb
                    p1 = TOK("vector", lambda e, bkq=bkq: e.tensor_mul(
                        scr[:, 1024:1536], scr[:, 512:1024], bkq.t[:]),
                        [mmb, prev_qs])
                    bkq.r = p1
                    p2 = TOK("scalar", lambda e: e.activation(
                        scr[:, 1024:1536], scr[:, 1024:1536], AF.Ln, bias=0.0, scale=1.0), [p1])
                    p3 = TOK("vector", lambda e: e.scalar_tensor_tensor(
                        scr[:, 1024:1536], scr[:, 1536:2048], SCALE, scr[:, 1024:1536],
                        OP.mult, OP.subtract), [p2, q1])
                    prev_p3 = p3
                    t_aqs[fi] = TOK("scalar", lambda e, fi=fi: e.activation(
                        aqs[:, fi * 512:(fi + 1) * 512], scr[:, 1024:1536], AF.Exp,
                        bias=0.0, scale=1.0), [p3, mmd])
                    prev_qs = t_aqs[fi]
                t_vT = [None] * NF
                for fi in range(NF):
                    sl = fi % 2
                    t_wl = DMA(wb1[:, sl * 1024:(sl + 1) * 1024], wvms_h[fi],
                               [t_w_rd[sl], t_staged["wvm"]], q="sync")
                    bk = bEv[fi % 2]
                    mm = None
                    for k in range(NF):
                        mm = TOK("tensor", lambda e, k=k, sl=sl, c=c, bk=bk: e.matmul(
                            bk.t[:], wb1[:, sl * 1024 + k * 128: sl * 1024 + (k + 1) * 128],
                            xlnT[:, k * 2048 + c * 512: k * 2048 + (c + 1) * 512],
                            start=(k == 0), stop=(k == NF - 1)),
                            ([t_wl, bk.r] + t_trf) if k == 0 else [])
                    bk.w = mm
                    t_w_rd[sl] = mm
                    ev = TOK("scalar", lambda e, fi=fi, bk=bk: e.copy(
                        vT[:, fi * 512:(fi + 1) * 512], bk.t[:]),
                        [bk.w] + ([t_buf_rd] if fi == 0 else []))
                    bk.r = ev
                    t_vT[fi] = ev

                def tp(srcbuf, dstbuf, deps_all):
                    evs = []
                    dfull = dstbuf.rearrange("p (st fj q) -> p st fj q", st=4, fj=8)
                    pin = pbT_t[:].rearrange("p (j st q) -> p st j q", j=2, st=4)
                    for f0 in range(0, NF, 2):
                        mm = None
                        for bi in range(8):
                            fi2 = f0 + bi // 4
                            st = bi % 4
                            mm = TOK("tensor", lambda e, fi2=fi2, st=st, bi=bi, srcbuf=srcbuf:
                                     e.transpose(
                                         pbT_t[:, bi * 128:(bi + 1) * 128],
                                         srcbuf[:, fi2 * 512 + st * 128: fi2 * 512 + (st + 1) * 128],
                                         identb[:]),
                                     ([bT.r] + deps_all) if bi == 0 else [])
                        bT.w = mm
                        dap = dfull[:, :, f0:f0 + 2, :]
                        ev = TOK("scalar", lambda e, dap=dap, pin=pin: e.copy(dap, pin), [bT.w])
                        bT.r = ev
                        evs.append(ev)
                    return evs
                evk = tp(akT, akt, t_akT)
                evv = tp(vT, av, t_vT)
                t_kv = p.fence("scalar")
                for h in range(HEADS):
                    pi = h % 2
                    fi = h // 2
                    ro = 64 * pi
                    mmA = None
                    for s in range(NST):
                        mmA = TOK("tensor", lambda e, s=s, fi=fi, ro=ro: e.matmul(
                            bA.t[:, s * 128:(s + 1) * 128],
                            akT[ro:ro + 64, fi * 512 + s * 128: fi * 512 + (s + 1) * 128],
                            aqs[ro:ro + 64, fi * 512 + s * 128: fi * 512 + (s + 1) * 128],
                            start=True, stop=True, skip_group_check=True),
                            ([bA.r, t_aqs[fi], t_kv] if s == 0 else []))
                    bA.w = mmA
                    am = amask[:, pi * 512:(pi + 1) * 512]
                    tA = TOK("vector", lambda e, am=am, bA=bA: e.tensor_mul(
                        am, bA.t[:], trimask[:]), [bA.w, tok_consts, t_amask_rd[pi]])
                    bA.r = tA
                    sb = bS[h // 8]
                    scol = (h % 8) * 64
                    ssl = ssb[ro:ro + 64, pi * 64:(pi + 1) * 64]
                    yb = bY[pi]
                    for s in range(NST):
                        first = (c == 0 and s == 0)
                        if not first:
                            cp = TOK("scalar", lambda e, sb=sb, scol=scol, ro=ro, ssl=ssl: e.copy(
                                ssl, sb.t[ro:ro + 64, scol:scol + 64]),
                                [tok_S_w[h], t_ssb_rd[pi]])
                            tok_S_r[h] = cp
                        st_tok = TOK("tensor",
                                     lambda e, s=s, h=h, sb=sb, scol=scol, ro=ro,
                                     st0=(not s_started[h // 8]): e.matmul(
                                         sb.t[ro:ro + 64, scol:scol + 64],
                                         akt[:, s * 1024 + h * 64: s * 1024 + (h + 1) * 64],
                                         av[:, s * 1024 + h * 64: s * 1024 + (h + 1) * 64],
                                         start=st0, stop=False, skip_group_check=True),
                                     [tok_S_r[h], t_kv] if not first else [sb.r, t_kv])
                        s_started[h // 8] = True
                        tok_S_w[h] = st_tok
                        sb.w = st_tok
                        dd = [tA, yb.r] if s == 0 else [tA]
                        y1 = TOK("tensor", lambda e, s=s, h=h, yb=yb, ro=ro, am=am, first=first:
                                 e.matmul(yb.t[ro:ro + 64, s * 128:(s + 1) * 128],
                                          av[:, s * 1024 + h * 64: s * 1024 + (h + 1) * 64],
                                          am[:, s * 128:(s + 1) * 128],
                                          start=True, stop=first, skip_group_check=True), dd)
                        yb.w = y1
                        if not first:
                            y2 = TOK("tensor", lambda e, s=s, fi=fi, yb=yb, ro=ro, ssl=ssl:
                                     e.matmul(yb.t[ro:ro + 64, s * 128:(s + 1) * 128],
                                              ssl,
                                              aqs[ro:ro + 64, fi * 512 + s * 128:
                                                  fi * 512 + (s + 1) * 128],
                                              start=False, stop=True, skip_group_check=True),
                                     [tok_S_r[h]])
                            yb.w = y2
                    ye = TOK("scalar", lambda e, yb=yb, ro=ro, fi=fi: e.copy(
                        ayT[ro:ro + 64, fi * 512:(fi + 1) * 512], yb.t[ro:ro + 64, :]),
                        [yb.w, t_ayT_rd])
                    yb.r = ye
                    t_amask_rd[pi] = p.fence("tensor")
                    t_ssb_rd[pi] = p.fence("tensor")
                t_Y_all = p.fence("scalar")
                for oh in range(2):
                    t_wo = [None] * NF
                    last_mm = [None] * NST
                    for k in range(NF):
                        sl = k % 2
                        t_wo[k] = DMA(wvo[:, sl * 512:(sl + 1) * 512], wos_h[oh, k],
                                      [t_wo_rd[sl], t_staged["wo"]], q="sync")
                        for st in range(NST):
                            bk = bDel[st]
                            dd = []
                            if k == 0:
                                dd = [t_wo[k], bk.r, t_Y_all]
                            elif st == 0:
                                dd = [t_wo[k]]
                            last_mm[st] = TOK("tensor", lambda e, k=k, st=st, sl=sl, bk=bk:
                                e.matmul(bk.t[:],
                                         ayT[:, k * 512 + st * 128: k * 512 + (st + 1) * 128],
                                         wvo[:, sl * 512:(sl + 1) * 512],
                                         start=(k == 0), stop=(k == NF - 1),
                                         skip_group_check=True), dd)
                        t_wo_rd[sl] = last_mm[NST - 1]
                    for st in range(NST):
                        bk = bDel[st]
                        bk.w = last_mm[st]
                        tt = 4 * c + st
                        colx = tt * 1024 + oh * 512
                        ra = TOK("vector", lambda e, colx=colx, bk=bk: e.tensor_add(
                            xres[:, colx:colx + 512], xres[:, colx:colx + 512], bk.t[:]),
                            [bk.w, tok_x[tt]])
                        bk.r = ra
                        if oh == 1:
                            tok_x[tt] = ra
                t_ayT_rd = p.fence("tensor")
                t_buf_rd = p.fence("tensor")

        for ph in phases:
            if ph == "F":
                ffn_sublayer()
            elif ph == "A":
                attn_sublayer()
            elif ph == "L":
                ffn_sublayer(upto="ln")
            elif ph == "N":
                ffn_sublayer(upto="ln", no_tr=True)
            elif ph == "C":
                ffn_sublayer(upto="cnt")

        st_toks = []
        for i in range(4):
            dst = out_h.rearrange("(t p) f -> p t f", p=128)[:, 4 * i:4 * i + 4, :]
            src = xres[:, 4 * i * 1024:(4 * i + 4) * 1024].rearrange("p (t f) -> p t f", f=1024)
            st_toks.append(DMA(dst, src, [tok_x[4 * i + j] for j in range(4)]))

        if dump is not None:
            bufmap = {"xlnb": (xlnb, BF16), "xlnT": (xlnT, BF16), "hT": (hT, BF16),
                      "scr": (scr, F32), "scor": (scor, F32), "hfm": (hfm, F32),
                      "csg": (csg, F32), "xres": (xres, F32), "mus": (mus, F32),
                      "rsd": (rsd, F32)}
            buf, dt = bufmap[dump]
            bar = p.barrier()
            if dt == F32:
                n = {"scr": 2048, "scor": 16, "hfm": 16, "csg": 16, "xres": 16384,
                     "mus": 16, "rsd": 16}[dump]
                DMA(dbg_h[:, 0:n], buf[:, 0:n], bar)
            else:
                nswept = 16384
                toks = bar
                for j0 in range(0, nswept, 2048):
                    cv = TOK("vector", lambda e, j0=j0: e.tensor_copy(
                        scr[:], buf[:, j0:j0 + 2048]), toks)
                    dtok = DMA(dbg_h[:, j0:j0 + 2048], scr[:], [cv])
                    toks = [dtok]

        final_sync = p.cnt["sync"]

        def emit(e, name):
            hi = {k: 0 for k in ENGINES}
            for w, fn, inc in p.steps[name]:
                for de, dv in w.items():
                    if dv > hi[de]:
                        e.wait_ge(sems[de], dv)
                        hi[de] = dv
                ins = fn(e)
                if name == "vector":
                    ins = e.drain()
                ins.then_inc(sems[name], inc)

        @block.sync
        def _(e):
            emit(e, "sync")

        @block.tensor
        def _(e):
            emit(e, "tensor")

        @block.vector
        def _(e):
            emit(e, "vector")

        @block.scalar
        def _(e):
            emit(e, "scalar")

        @block.gpsimd
        def _(e):
            emit(e, "gpsimd")
            e.wait_ge(sems["sync"], final_sync)

    mybir.codegen_inst_isa_subclasses(nc)
    return nc


def prep_inputs(x_b, w, w_out, w_r, b_r, W1, b1, W2, b2):
    f32 = np.float32
    bf16 = ml_dtypes.bfloat16
    wT = np.ascontiguousarray(w.T)
    wqk = np.ascontiguousarray(
        wT[:, :2048].reshape(8, 128, 16, 128).transpose(2, 1, 0, 3).reshape(16, 128, 1024))
    wvm = np.ascontiguousarray(
        wT[:, 2048:].reshape(8, 128, 8, 128).transpose(2, 1, 0, 3).reshape(8, 128, 1024))
    wo = np.ascontiguousarray(
        np.asarray(w_out).reshape(8, 128, 2, 512).transpose(2, 0, 1, 3))
    w1 = np.ascontiguousarray(
        np.asarray(W1).reshape(8, 128, 32, 128).transpose(2, 1, 0, 3).reshape(32, 128, 1024))
    w2 = np.ascontiguousarray(
        np.asarray(W2).reshape(32, 128, 2, 512).transpose(2, 0, 1, 3))
    b1c = np.ascontiguousarray(np.asarray(b1).reshape(32, 128).T)
    b2r = np.ascontiguousarray(np.asarray(b2).reshape(1, 1024))
    wrb = np.ascontiguousarray(np.broadcast_to(np.asarray(w_r).reshape(1, 1024), (128, 1024)))
    brb = np.ascontiguousarray(np.broadcast_to(np.asarray(b_r).reshape(1, 1), (128, 1)))
    identb = np.eye(128, dtype=f32).astype(bf16)
    tri = np.triu(np.ones((128, 128), f32))
    trimask = np.ascontiguousarray(np.tile(tri, (1, 4))).astype(bf16)
    ones1f = np.ones((1, 128), f32)
    ones2 = np.ones((2, 128), f32).astype(bf16)
    ind2 = np.zeros((128, 2), f32)
    ind2[0:64, 0] = 1.0
    ind2[64:128, 1] = 1.0
    return {
        "x": np.ascontiguousarray(np.asarray(x_b), f32), "wqk": wqk, "wvm": wvm,
        "wo": wo, "w1": w1, "w2": w2,
        "b1c": b1c, "b2r": b2r, "wrb": wrb, "brb": brb,
        "identb": identb, "trimask": trimask, "ones1f": ones1f, "ones2": ones2,
        "ind2": ind2.astype(bf16), "ind2T": np.ascontiguousarray(ind2.T).astype(bf16),
        "onesK1": np.ones((1, 128), f32).astype(bf16),
    }


def kernel(L0_attn_x, L0_attn_w, L0_attn_w_out, L0_ffn_w_r, L0_ffn_b_r,
           L0_ffn_W1, L0_ffn_b1, L0_ffn_W2, L0_ffn_b2):
    x = np.asarray(L0_attn_x, dtype=np.float32)
    nc = build(phases="FAF")
    shared = prep_inputs(x[0], np.asarray(L0_attn_w), np.asarray(L0_attn_w_out),
                         np.asarray(L0_ffn_w_r), np.asarray(L0_ffn_b_r),
                         np.asarray(L0_ffn_W1), np.asarray(L0_ffn_b1),
                         np.asarray(L0_ffn_W2), np.asarray(L0_ffn_b2))
    in_maps = []
    for b in range(8):
        im = dict(shared)
        im["x"] = np.ascontiguousarray(x[b], np.float32)
        in_maps.append(im)
    res = run_bass_kernel_spmd(nc, in_maps, core_ids=list(range(8)))
    out = np.stack([np.asarray(res.results[b]["out"]) for b in range(8)], axis=0)
    return out.astype(np.float32)
